# version 19
# baseline (speedup 1.0000x reference)
import numpy as np
from contextlib import ExitStack

import concourse.bass as bass
import concourse.mybir as mybir
from concourse.bass_utils import run_bass_kernel_spmd

F32 = mybir.dt.float32
BF16 = mybir.dt.bfloat16
I32 = mybir.dt.int32
AF = mybir.ActivationFunctionType
ALU = mybir.AluOpType
AX = mybir.AxisListType

D = 4096
TOK = 1024
KT = 32
NT = 8
NCH = 8
NW = 512
ER = 512
EPS = 1e-5
SC_SCALE = 1.0 / (4096.0 * float(np.sqrt(128.0)))


class Plan:

    def __init__(self):
        self.ev = {}
        self.emit = False
        self.cnt = {}
        self.eng = {}
        self.sem = {}

    def inc(self, sem_name, instr, amount=1):
        self.cnt[sem_name] = self.cnt.get(sem_name, 0) + amount
        if self.emit and instr is not None:
            instr.then_inc(self.sem[sem_name], amount)
        return self.cnt[sem_name]

    def mark(self, sem_name, name, instr=None, amount=1):
        v = self.inc(sem_name, instr, amount)
        if not self.emit:
            self.ev[name] = (sem_name, v)
        return v

    def wait(self, eng, name):
        if self.emit:
            s, v = self.ev[name]
            eng.wait_ge(self.sem[s], v)

    def wait_val(self, eng, sem_name, v):
        if self.emit:
            eng.wait_ge(self.sem[sem_name], v)


def build_program():
    nc = bass.Bass("TRN2", target_bir_lowering=False)
    xT = nc.dram_tensor("xT", [D, TOK], F32, kind="ExternalInput").ap()
    xres = nc.dram_tensor("xres", [TOK, D], F32, kind="ExternalInput").ap()
    wT = nc.dram_tensor("wT", [D, D], F32, kind="ExternalInput").ap()
    orgT = nc.dram_tensor("orgT", [D, D], F32, kind="ExternalInput").ap()
    down = nc.dram_tensor("down", [D, ER], F32, kind="ExternalInput").ap()
    up = nc.dram_tensor("up", [ER, D], F32, kind="ExternalInput").ap()
    wqT = nc.dram_tensor("wqT", [D, 128], F32, kind="ExternalInput").ap()
    keysT = nc.dram_tensor("keysT", [128, 8], F32, kind="ExternalInput").ap()
    bias = nc.dram_tensor("bias", [1, D], F32, kind="ExternalInput").ap()
    bmTs = nc.dram_tensor("bmTs", [128, 128], I32, kind="ExternalInput").ap()
    out = nc.dram_tensor("out", [TOK, D], F32, kind="ExternalOutput").ap()
    cc_in = nc.dram_tensor("cc_in", [128, KT], F32)
    cc_out = nc.dram_tensor("cc_out", [128, KT], F32)

    xTr = xT.rearrange("(a p) t -> p a t", p=128)
    wTr = wT.rearrange("(a p) n -> p a n", p=128)
    orgTr = orgT.rearrange("(a p) n -> p a n", p=128)
    downr = down.rearrange("(a p) r -> p a r", p=128)
    upr = up.rearrange("(a p) n -> p a n", p=128)
    wqTr = wqT.rearrange("(a p) q -> p a q", p=128)

    P = Plan()
    ctx = ExitStack()
    with ctx:
        E = ctx.enter_context
        xc = E(nc.sbuf_tensor("xc", [128, KT * TOK], BF16))
        wA = E(nc.sbuf_tensor("wA", [128, KT * NW], BF16))
        wO = E(nc.sbuf_tensor("wO", [128, KT * NW], BF16))
        ub = [E(nc.sbuf_tensor(f"ub{i}", [128, 4 * NW], BF16)) for i in range(2)]
        db = [E(nc.sbuf_tensor(f"db{i}", [128, NW], BF16)) for i in range(3)]
        hz = E(nc.sbuf_tensor("hz", [128, 4 * TOK], BF16))
        bme = E(nc.sbuf_tensor("bme", [128, NW], BF16))
        bias_sm = E(nc.sbuf_tensor("bias_sm", [128, NW], BF16))
        m_row_bf = E(nc.sbuf_tensor("m_row_bf", [1, TOK], BF16))
        r_row_bf = E(nc.sbuf_tensor("r_row_bf", [1, TOK], BF16))
        co_row_bf = E(nc.sbuf_tensor("co_row_bf", [1, 8], BF16))
        wq_sb = E(nc.sbuf_tensor("wq_sb", [128, KT * 128], BF16))
        m_bc = E(nc.sbuf_tensor("m_bc", [128, TOK], BF16))
        r_bc = E(nc.sbuf_tensor("r_bc", [128, TOK], BF16))
        xr_sb = [E(nc.sbuf_tensor(f"xr{i}", [128, NW], F32)) for i in range(2)]
        s_sb = [E(nc.sbuf_tensor(f"ss{i}", [128, NW], BF16)) for i in range(8)]
        o_sb = [E(nc.sbuf_tensor(f"os{i}", [128, NW], F32)) for i in range(2)]
        g_sb = [E(nc.sbuf_tensor(f"gs{i}", [128, NW], F32)) for i in range(2)]
        cs = E(nc.sbuf_tensor("cs", [128, KT], F32))
        csr = E(nc.sbuf_tensor("csr", [128, KT], F32))
        csr_bf = E(nc.sbuf_tensor("csr_bf", [128, KT], BF16))
        r_col = E(nc.sbuf_tensor("r_col", [128, NT], F32))
        q_bf = E(nc.sbuf_tensor("q_bf", [128, 1], BF16))
        keys_f = E(nc.sbuf_tensor("keys_f", [128, 8], F32))
        keys_bf = E(nc.sbuf_tensor("keys_bf", [128, 8], BF16))
        bmT_i = E(nc.sbuf_tensor("bmT_i", [128, 128], I32))
        bmT_bf = E(nc.sbuf_tensor("bmT_bf", [128, 128], BF16))
        c8 = E(nc.sbuf_tensor("c8", [128, 8], F32))
        c_vec = E(nc.sbuf_tensor("c_vec", [128, 4], F32))
        ones_cb = E(nc.sbuf_tensor("ones_cb", [128, 1], BF16))
        bias_f = E(nc.sbuf_tensor("bias_f", [1, D], BF16))
        ones_rb = E(nc.sbuf_tensor("ones_rb", [1, 128], BF16))
        m_row = E(nc.sbuf_tensor("m_row", [1, TOK], F32))
        r_row = E(nc.sbuf_tensor("r_row", [1, TOK], F32))
        rowB = E(nc.sbuf_tensor("rowB", [1, TOK], F32))
        eps_sb = E(nc.sbuf_tensor("eps_sb", [1, 1], F32))
        m2_row = rowB
        var_row = m_row
        s0_row = rowB
        rec_row = r_row
        tn_row = rowB
        u_row = rowB
        s1_row = rowB
        sc_row = E(nc.sbuf_tensor("sc_row", [1, 8], F32))
        sc2_row = E(nc.sbuf_tensor("sc2_row", [1, 8], F32))
        mx_row = E(nc.sbuf_tensor("mx_row", [1, 1], F32))
        e_row = E(nc.sbuf_tensor("e_row", [1, 8], F32))
        sm_row = E(nc.sbuf_tensor("sm_row", [1, 1], F32))
        co_row = E(nc.sbuf_tensor("co_row", [1, 8], F32))
        ps = [E(nc.psum_tensor(f"ps{i}", [128, NW], F32)) for i in range(8)]
        ps_s = ps[0:3]
        ps_f = ps[3:6]
        ps_x = ps[6:8]
        sem_names = ["t", "v", "s", "g", "cc", "d_misc", "d_dn0", "d_dn1", "d_dn2",
                     "d_w", "d_o", "d_u0", "d_u1", "d_r0", "d_r1",
                     "d_st0", "d_st1", "d_cs", "d_csr", "d_rc", "d_wq",
                     "d_xc0", "d_xc1", "d_xc2", "d_xc3"]
        for sn in sem_names:
            P.sem[sn] = E(nc.semaphore(sn))
        block = E(nc.Block())

        def h_piece_ids():
            return list(range(32))

        def gp_stream2(g):
            em = P.emit

            def dma(sem, o, i):
                ins = g.dma_start(out=o, in_=i) if em else None
                return P.inc(sem, ins, 16)

            for qd in range(4):
                dma(f"d_xc{qd}", xc[:, qd * 8 * TOK:(qd + 1) * 8 * TOK],
                    xTr[:, qd * 8:(qd + 1) * 8, :])
            dma("d_misc", bmT_i[:, :], bmTs[:, :])
            dma("d_misc", bias_f[:, :], bias[:, :])
            dma("d_misc", keys_f[:, :], keysT[:, :])
            if not P.emit:
                P.ev["misc_ld"] = ("d_misc", P.cnt["d_misc"])
            for qd in range(4):
                dma("d_w", wA[:, qd * 8 * NW:(qd + 1) * 8 * NW],
                    wTr[:, qd * 8:(qd + 1) * 8, 0:NW])
            P.wait(g, "v_rrow")
            for tt in range(NT):
                dma("d_rc", r_col[:, tt:tt + 1], r_row[0:1, tt * 128:(tt + 1) * 128])
            for qd in range(4):
                dma("d_o", wO[:, qd * 8 * NW:(qd + 1) * 8 * NW],
                    orgTr[:, qd * 8:(qd + 1) * 8, 0:NW])
            dma("d_u0", ub[0][:, :], upr[:, :, 0:NW])
            dma("d_wq", wq_sb[:, :], wqTr[:, :, :])
            P.wait(g, "v_cs")
            dma("d_cs", cc_in[:, :], cs[:, :])
            P.wait_val(g, "d_cs", 16)
            ins = g.collective_compute(
                "AllReduce", ALU.add,
                replica_groups=[[0, 1, 2, 3], [4, 5, 6, 7]],
                ins=[cc_in[:, :].opt()],
                outs=[cc_out[:, :].opt()],
            ) if em else None
            P.mark("cc", "cc_done", ins)
            for i in h_piece_ids():
                if i >= 3:
                    P.wait(g, f"t_hpiece{i - 3}")
                dma(f"d_dn{i % 3}", db[i % 3][:, :],
                    downr[:, i:i + 1, :])
            for j in range(1, NCH):
                P.wait(g, f"t_attn_last{j - 1}")
                for qd in range(4):
                    dma("d_w", wA[:, qd * 8 * NW:(qd + 1) * 8 * NW],
                        wTr[:, qd * 8:(qd + 1) * 8, j * NW:(j + 1) * NW])
                P.wait(g, f"t_ffn_last{j - 1}")
                for qd in range(4):
                    dma("d_o", wO[:, qd * 8 * NW:(qd + 1) * 8 * NW],
                        orgTr[:, qd * 8:(qd + 1) * 8, j * NW:(j + 1) * NW])
                if j >= 2:
                    P.wait(g, f"t_ffn_last{j - 2}")
                dma(f"d_u{j % 2}", ub[j % 2][:, :], upr[:, :, j * NW:(j + 1) * NW])

        def t_stream(t):
            em = P.emit

            def mm(out_ap, lhsT, rhs, start, stop):
                return t.matmul(out_ap, lhsT, rhs, start=start, stop=stop) if em else None

            P.wait(t, "v_ones")
            for kt in range(KT):
                if kt % 8 == 0:
                    P.wait_val(t, f"d_xc{kt // 8}", 16)
                for c in range(2):
                    ins = mm(ps_x[c][0:1, 0:NW], ones_cb[:, :],
                             xc[:, kt * TOK + c * NW: kt * TOK + (c + 1) * NW],
                             kt == 0, kt == KT - 1)
                if kt == KT - 1:
                    P.mark("t", "t_msum", ins)
                P.wait(t, f"v_sq{kt}")
                for c in range(2):
                    ins = mm(ps_f[c][0:1, 0:NW], ones_cb[:, :],
                             hz[:, (kt % 4) * TOK + c * NW:(kt % 4) * TOK + (c + 1) * NW],
                             kt == 0, kt == KT - 1)
                P.mark("t", f"t_sq{kt}", ins)
            P.wait(t, "v_mrowbf")
            for c in range(2):
                ins = mm(ps_s[c][:, :], ones_rb[:, :], m_row_bf[0:1, c * NW:(c + 1) * NW],
                         True, True)
                P.mark("t", f"t_mbc{c}", ins)
            P.wait(t, "misc_ld")
            P.wait(t, "v_mrow")
            ins = mm(ps_x[0][:, :], ones_rb[:, :], bias_f[0:1, 0:NW], True, True)
            P.mark("t", "t_biasj0", ins)
            for tt in range(NT):
                idx = tt
                if tt == 3:
                    P.wait(t, "v_rrowbf")
                    P.wait(t, "v_mbc")
                    P.wait(t, "v_biasev0")
                    for c in range(2):
                        ins = mm(ps_x[c][:, :], ones_rb[:, :],
                                 r_row_bf[0:1, c * NW:(c + 1) * NW], True, True)
                        P.mark("t", f"t_rbc{c}", ins)
                if idx >= 3:
                    P.wait(t, f"v_sstt{idx - 3}")
                if tt >= 1:
                    P.wait(t, f"v_center_tt{tt}")
                ins = None
                for kt in range(KT):
                    if tt == 0:
                        P.wait(t, f"v_pk{kt}")
                    ins = mm(ps_s[idx % 3][:, :],
                             xc[:, kt * TOK + tt * 128: kt * TOK + (tt + 1) * 128],
                             wA[:, kt * NW:(kt + 1) * NW], kt == 0, kt == KT - 1)
                P.mark("t", f"t_attn_0_{tt}", ins)
            if not P.emit:
                P.ev["t_attn_last0"] = P.ev["t_attn_0_7"]
            P.wait(t, "v_sstt7")
            P.wait(t, "v_var")
            P.wait(t, "v_rbc")
            for i in h_piece_ids():
                P.wait_val(t, f"d_dn{i % 3}", 16 * (i // 3 + 1))
                kt = i
                ins = None
                for gg in range(8):
                    et = gg // 2
                    half = gg % 2
                    ins = mm(ps[gg][:, :],
                             db[i % 3][:, et * 128: et * 128 + 128],
                             xc[:, kt * TOK + half * NW: kt * TOK + (half + 1) * NW],
                             (i == 0), (i == 31))
                P.mark("t", f"t_hpiece{i}", ins)
            P.wait(t, "v_csrbf")
            P.wait(t, "v_hz_6")
            P.wait(t, "v_hz_7")
            P.wait(t, "v_biasev0")
            P.wait_val(t, "d_wq", 16)
            ins = None
            for kt in range(KT):
                ins = mm(ps_x[0][:, 0:1], wq_sb[:, kt * 128:(kt + 1) * 128],
                         csr_bf[:, kt:kt + 1], kt == 0, kt == KT - 1)
            P.mark("t", "t_q", ins)
            def org_group(tt):
                idx = tt
                if tt == 0:
                    P.wait_val(t, "d_o", 64)
                    P.wait_val(t, "d_u0", 16)
                P.wait(t, f"v_hz_{3 + tt}")
                ins = None
                for kt in range(KT):
                    ins = mm(ps_f[idx % 3][:, :],
                             xc[:, kt * TOK + tt * 128: kt * TOK + (tt + 1) * 128],
                             wO[:, kt * NW:(kt + 1) * NW], kt == 0, False)
                return ins

            org_group(0)
            P.wait(t, "v_qbf")
            ins = mm(ps_x[1][0:1, 0:8], q_bf[:, :], keys_bf[:, :], True, True)
            P.mark("t", "t_sc", ins)
            org_group(1)
            P.wait(t, "v_cobf")
            ins = mm(ps_x[0][:, 0:8], ones_rb[:, :], co_row_bf[:, :], True, True)
            P.mark("t", "t_c8", ins)
            org_group(2)
            P.wait(t, "v_zT")
            for tt in range(3):
                idx = tt
                ins = None
                for et in range(4):
                    ins = mm(ps_f[idx % 3][:, :],
                             hz[:, et * TOK + tt * 128: et * TOK + (tt + 1) * 128],
                             ub[0][:, et * NW:(et + 1) * NW], False, et == 3)
                P.mark("t", f"t_ffn_0_{tt}", ins)
            for j in range(NCH):
                if j >= 1:
                    if j == 1:
                        P.wait(t, "v_scrow")
                    elif j == 2:
                        P.wait(t, "v_c8")
                    else:
                        P.wait(t, f"v_biasev{j - 2}")
                    ins = mm(ps_x[j % 2][:, :], ones_rb[:, :],
                             bias_f[0:1, j * NW:(j + 1) * NW], True, True)
                    P.mark("t", f"t_biasj{j}", ins)
                    P.wait(t, f"v_mask{j}")
                    for tt in range(NT):
                        idx = 8 * j + tt
                        if j == 1 and tt < 3:
                            P.wait(t, f"v_hz_{(8 + tt) % 3}")
                        if idx >= 3:
                            P.wait(t, f"v_sstt{idx - 3}")
                        ins = None
                        for kt in range(KT):
                            ins = mm(ps_s[idx % 3][:, :],
                                     xc[:, kt * TOK + tt * 128: kt * TOK + (tt + 1) * 128],
                                     wA[:, kt * NW:(kt + 1) * NW], kt == 0, kt == KT - 1)
                        P.mark("t", f"t_attn_{j}_{tt}", ins)
                for tt in range(NT):
                    idx = 8 * j + tt
                    if j == 0 and tt < 3:
                        continue
                    if tt == 0:
                        P.wait_val(t, "d_o", 64 * (j + 1))
                        P.wait_val(t, f"d_u{j % 2}", 16 * (j // 2 + 1))
                    if idx >= 3:
                        P.wait(t, f"s_silu{idx - 3}")
                    ins = None
                    for kt in range(KT):
                        ins = mm(ps_f[idx % 3][:, :],
                                 xc[:, kt * TOK + tt * 128: kt * TOK + (tt + 1) * 128],
                                 wO[:, kt * NW:(kt + 1) * NW], kt == 0, False)
                    for et in range(4):
                        ins = mm(ps_f[idx % 3][:, :],
                                 hz[:, et * TOK + tt * 128: et * TOK + (tt + 1) * 128],
                                 ub[j % 2][:, et * NW:(et + 1) * NW], False, et == 3)
                    P.mark("t", f"t_ffn_{j}_{tt}", ins)
                if not P.emit:
                    if j >= 1:
                        P.ev[f"t_attn_last{j}"] = P.ev[f"t_attn_{j}_{NT - 1}"]
                    P.ev[f"t_ffn_last{j}"] = P.ev[f"t_ffn_{j}_{NT - 1}"]

        def v_stream(v):
            em = P.emit

            def vi(f, *a, **kw):
                return f(*a, **kw) if em else None

            ins = vi(v.memset, ones_cb[:, :], 1.0)
            ins = vi(v.memset, ones_rb[:, :], 1.0)
            ins = vi(v.memset, eps_sb[:, :], EPS)
            ins = vi(v.engine_nop) if em else None
            P.mark("v", "v_ones", ins)
            for kt in range(KT):
                if kt % 8 == 0:
                    P.wait_val(v, f"d_xc{kt // 8}", 16)
                if kt >= 4:
                    P.wait(v, f"t_sq{kt - 4}")
                ins = vi(v.tensor_tensor, hz[:, (kt % 4) * TOK:(kt % 4 + 1) * TOK],
                         xc[:, kt * TOK:(kt + 1) * TOK],
                         xc[:, kt * TOK:(kt + 1) * TOK], ALU.mult)
                P.mark("v", f"v_sq{kt}", ins)
            P.wait(v, "t_msum")
            for c in range(2):
                ins = vi(v.tensor_scalar, m_row[0:1, c * NW:(c + 1) * NW],
                         ps_x[c][0:1, 0:NW], 1.0 / 4096.0, None, ALU.mult)
            P.mark("v", "v_mrow", ins)
            P.wait(v, "v_mrow")
            ins = vi(v.tensor_copy, m_row_bf[:, :], m_row[:, :])
            P.mark("v", "v_mrowbf", ins)
            ins = vi(v.tensor_tensor, m2_row[:, :], m_row[:, :], m_row[:, :], ALU.mult)
            P.mark("v", "v_m2", ins)
            P.wait(v, f"t_sq{KT - 1}")
            P.wait(v, "t_mbc1")
            P.wait(v, "v_m2")
            for c in range(2):
                ins = vi(v.scalar_tensor_tensor, var_row[:, c * NW:(c + 1) * NW],
                         ps_f[c][0:1, 0:NW], 1.0 / 4096.0,
                         m2_row[:, c * NW:(c + 1) * NW], ALU.mult, ALU.subtract)
            P.mark("v", "v_var", ins)

            P.wait(v, "t_mbc0")
            P.wait(v, "t_mbc1")
            for c in range(2):
                ins = vi(v.tensor_copy, m_bc[:, c * NW:(c + 1) * NW], ps_s[c][:, :])
            P.mark("v", "v_mbc", ins)
            P.wait(v, "misc_ld")
            ins = vi(v.tensor_single_scalar, bmT_bf[:, :], bmT_i[:, :], 0, ALU.is_gt)
            P.mark("v", "v_bmt", ins)
            P.wait(v, "v_bmt")
            for jj in range(32):
                ins = vi(v.tensor_copy, bme[:, jj::32], bmT_bf[:, 0:16])
            P.mark("v", "v_bme", ins)
            ins = vi(v.tensor_copy, keys_bf[:, :], keys_f[:, :])
            P.mark("v", "v_keys", ins)
            P.wait(v, "v_mbc")
            P.wait(v, "v_bme")
            P.wait_val(v, "d_w", 64)
            for kt in range(KT):
                ins = vi(v.tensor_tensor, xc[:, kt * TOK:kt * TOK + 128],
                         xc[:, kt * TOK:kt * TOK + 128], m_bc[:, 0:128],
                         ALU.subtract)
                P.inc("v", ins)
                ins = vi(v.tensor_tensor, wA[:, kt * NW:(kt + 1) * NW],
                         wA[:, kt * NW:(kt + 1) * NW], bme[:, :], ALU.mult)
                P.mark("v", f"v_pk{kt}", ins)
            P.wait(v, "t_biasj0")
            ins = vi(v.tensor_copy, bias_sm[:, :], ps_x[0][:, :])
            P.mark("v", "v_biasev0", ins)
            P.wait(v, "v_biasev0")

            def center_slice(tta, ttb=None):
                ttb = tta if ttb is None else ttb
                w = (ttb - tta + 1) * 128
                ins = None
                for kt in range(KT):
                    ins = vi(v.tensor_tensor,
                             xc[:, kt * TOK + tta * 128: kt * TOK + tta * 128 + w],
                             xc[:, kt * TOK + tta * 128: kt * TOK + tta * 128 + w],
                             m_bc[:, tta * 128: tta * 128 + w], ALU.subtract)
                    if kt < KT - 1:
                        P.inc("v", ins)
                P.mark("v", f"v_center_tt{ttb}", ins)
                if not P.emit:
                    for q in range(tta, ttb):
                        P.ev[f"v_center_tt{q}"] = P.ev[f"v_center_tt{ttb}"]

            def sstt0(tt):
                P.wait(v, f"t_attn_0_{tt}")
                if tt == 0:
                    P.wait_val(v, "d_rc", 16 * NT)
                ins = vi(v.scalar_tensor_tensor, s_sb[tt % 8][:, :],
                         ps_s[tt % 3][:, :], r_col[:, tt:tt + 1],
                         bias_sm[:, :], ALU.mult, ALU.add)
                P.mark("v", f"v_sstt{tt}", ins)

            center_slice(1, 2)
            P.wait(v, "s_sqrt")
            ins = vi(v.reciprocal, rec_row[:, :], s0_row[:, :])
            P.mark("v", "v_rec", ins)
            P.wait(v, "v_rec")
            ins = vi(v.tensor_tensor, tn_row[:, :], rec_row[:, :], rec_row[:, :], ALU.mult)
            P.mark("v", "v_tn", ins)
            P.wait(v, "v_tn")
            ins = vi(v.scalar_tensor_tensor, u_row[:, :], var_row[:, :], EPS,
                     tn_row[:, :], ALU.add, ALU.mult)
            P.mark("v", "v_u", ins)
            P.wait(v, "v_u")
            ins = vi(v.tensor_scalar, s1_row[:, :], u_row[:, :], -0.5, 1.5, ALU.mult, ALU.add)
            P.mark("v", "v_s1", ins)
            P.wait(v, "v_s1")
            ins = vi(v.tensor_tensor, r_row[:, :], rec_row[:, :], s1_row[:, :], ALU.mult)
            P.mark("v", "v_rrow", ins)
            P.wait(v, "v_rrow")
            ins = vi(v.tensor_copy, r_row_bf[:, :], r_row[:, :])
            P.mark("v", "v_rrowbf", ins)

            center_slice(3, 4)
            sstt0(0)
            sstt0(1)
            P.wait(v, "t_rbc0")
            P.wait(v, "t_rbc1")
            for c in range(2):
                ins = vi(v.tensor_copy, r_bc[:, c * NW:(c + 1) * NW], ps_x[c][:, :])
            P.mark("v", "v_rbc", ins)
            P.wait(v, "v_rbc")
            center_slice(5, 6)
            sstt0(2)
            sstt0(3)
            center_slice(7)
            sstt0(4)
            for kt in range(KT):
                if kt >= 2:
                    P.wait(v, f"v_cs{kt - 2}")
                ins = vi(v.scalar_tensor_tensor, hz[:, (2 + kt % 2) * TOK:(3 + kt % 2) * TOK],
                         xc[:, kt * TOK:(kt + 1) * TOK], 1.0, r_bc[:, :],
                         ALU.mult, ALU.mult, accum_out=cs[:, kt:kt + 1])
                P.mark("v", f"v_cs{kt}", ins)
                if kt % 8 == 7 and kt // 8 + 5 <= 7:
                    sstt0(kt // 8 + 5)
            if not P.emit:
                P.ev["v_cs"] = P.ev[f"v_cs{KT - 1}"]
            P.wait_val(v, "d_csr", 16)
            ins = vi(v.tensor_copy, csr_bf[:, :], csr[:, :])
            P.mark("v", "v_csrbf", ins)
            P.wait(v, "t_hpiece31")
            P.wait(v, "v_cs")
            for gg in range(8):
                et = gg // 2
                half = gg % 2
                ins = vi(v.tensor_copy, hz[:, et * TOK + half * NW: et * TOK + (half + 1) * NW],
                         ps[gg][:, :])
                P.mark("v", f"v_hz_{gg}", ins)
            P.wait(v, "t_q")
            ins = vi(v.tensor_copy, q_bf[:, :], ps_x[0][:, 0:1])
            P.mark("v", "v_qbf", ins)
            P.wait(v, "t_sc")
            ins = vi(v.tensor_scalar, sc_row[:, :], ps_x[1][0:1, 0:8], SC_SCALE, None, ALU.mult)
            P.mark("v", "v_scrow", ins)
            P.wait(v, "v_scrow")
            ins = vi(v.reduce_max, mx_row[:, :], sc_row[:, :], AX.X)
            P.mark("v", "v_mx", ins)
            P.wait(v, "v_mx")
            ins = vi(v.tensor_scalar, sc2_row[:, :], sc_row[:, :], mx_row[0:1, 0:1], None, ALU.subtract)
            P.mark("v", "v_sc2", ins)
            P.wait(v, "s_exp")
            ins = vi(v.reduce_sum, sm_row[:, :], e_row[:, :], AX.X)
            P.mark("v", "v_sm", ins)
            P.wait(v, "v_sm")
            ins = vi(v.reciprocal, sm_row[:, :], sm_row[:, :])
            P.mark("v", "v_smr", ins)
            P.wait(v, "v_smr")
            ins = vi(v.tensor_scalar, co_row[:, :], e_row[:, :], sm_row[0:1, 0:1], None, ALU.mult)
            P.mark("v", "v_coeffs", ins)
            P.wait(v, "v_coeffs")
            ins = vi(v.tensor_copy, co_row_bf[:, :], co_row[:, :])
            P.mark("v", "v_cobf", ins)
            P.wait(v, "t_c8")
            ins = vi(v.tensor_copy, c8[:, :], ps_x[0][:, 0:8])
            P.mark("v", "v_c8", ins)
            P.wait(v, "v_c8")
            for et in range(4):
                ins = vi(v.tensor_copy, c_vec[0:64, et:et + 1], c8[0:64, 2 * et:2 * et + 1])
                ins = vi(v.tensor_copy, c_vec[64:128, et:et + 1], c8[64:128, 2 * et + 1:2 * et + 2])
            P.mark("v", "v_cvec", ins)
            P.wait(v, "v_cvec")
            for et in range(4):
                P.wait(v, f"v_hz_{2 * et}")
                P.wait(v, f"v_hz_{2 * et + 1}")
                ins = vi(v.scalar_tensor_tensor, hz[:, et * TOK:(et + 1) * TOK],
                         hz[:, et * TOK:(et + 1) * TOK], c_vec[:, et:et + 1],
                         r_bc[:, :], ALU.mult, ALU.mult)
            P.mark("v", "v_zT", ins)
            P.wait_val(v, "d_w", 128)
            P.wait(v, f"v_pk{KT - 1}")
            for jj in range(32):
                ins = vi(v.tensor_copy, bme[:, jj::32], bmT_bf[:, 16:32])
            ins = vi(v.engine_nop) if P.emit else None
            P.mark("v", "v_bme1", ins)
            P.wait(v, "v_bme1")
            for kt in range(KT):
                ins = vi(v.tensor_tensor, wA[:, kt * NW:(kt + 1) * NW],
                         wA[:, kt * NW:(kt + 1) * NW], bme[:, :], ALU.mult)
            P.mark("v", "v_mask1", ins)
            for j in range(NCH):
                if j >= 1:
                    P.wait(v, f"t_biasj{j}")
                    P.wait(v, f"v_sstt{8 * (j - 1) + 7}")
                    ins = vi(v.tensor_copy, bias_sm[:, :], ps_x[j % 2][:, :])
                    P.mark("v", f"v_biasev{j}", ins)
                    P.wait(v, f"v_biasev{j}")
                    for tt in range(NT):
                        idx = 8 * j + tt
                        P.wait(v, f"t_attn_{j}_{tt}")
                        if idx >= 8:
                            P.wait(v, f"v_tmul{idx - 8}")
                        ins = vi(v.scalar_tensor_tensor, s_sb[idx % 8][:, :],
                                 ps_s[idx % 3][:, :], r_col[:, tt:tt + 1],
                                 bias_sm[:, :], ALU.mult, ALU.add)
                        P.mark("v", f"v_sstt{idx}", ins)

                def epi(tt):
                    idx = 8 * j + tt
                    P.wait(v, f"s_silu{idx}")
                    P.wait(v, f"v_sstt{idx}")
                    if idx >= 2:
                        P.wait_val(v, f"d_st{idx % 2}", 16 * (idx // 2))
                    ins = vi(v.tensor_tensor, o_sb[idx % 2][:, :], s_sb[idx % 8][:, :],
                             g_sb[idx % 2][:, :], ALU.mult)
                    P.mark("v", f"v_tmul{idx}", ins)
                    P.wait(v, f"v_tmul{idx}")
                    P.wait_val(v, f"d_r{idx % 2}", 16 * (idx // 2 + 1))
                    ins = vi(v.tensor_tensor, o_sb[idx % 2][:, :], o_sb[idx % 2][:, :],
                             xr_sb[idx % 2][:, :], ALU.add)
                    P.mark("v", f"v_outadd{idx}", ins)

                for tt in range(4):
                    epi(tt)
                if j + 1 < NCH and j >= 1:
                    P.wait_val(v, "d_w", 64 * (j + 2))
                    P.wait(v, f"v_mask{j}")
                    for jj in range(32):
                        ins = vi(v.tensor_copy, bme[:, jj::32],
                                 bmT_bf[:, (j + 1) * 16:(j + 2) * 16])
                    ins = vi(v.engine_nop) if P.emit else None
                    P.mark("v", f"v_bme{j + 1}", ins)
                    P.wait(v, f"v_bme{j + 1}")
                    for kt in range(KT):
                        ins = vi(v.tensor_tensor, wA[:, kt * NW:(kt + 1) * NW],
                                 wA[:, kt * NW:(kt + 1) * NW],
                                 bme[:, :], ALU.mult)
                    P.mark("v", f"v_mask{j + 1}", ins)
                for tt in range(4, NT):
                    epi(tt)

        def s_stream(s):
            em = P.emit
            P.wait(s, "v_var")
            ins = s.activation(s0_row[:, :], var_row[:, :], AF.Sqrt,
                               bias=eps_sb[0:1, 0:1]) if em else None
            P.mark("s", "s_sqrt", ins)
            P.wait(s, "v_sc2")
            ins = s.activation(e_row[:, :], sc2_row[:, :], AF.Exp) if em else None
            P.mark("s", "s_exp", ins)
            for j in range(NCH):
                for tt in range(NT):
                    idx = 8 * j + tt
                    P.wait(s, f"t_ffn_{j}_{tt}")
                    if idx >= 2:
                        P.wait(s, f"v_tmul{idx - 2}")
                    if idx == 0:
                        P.wait_val(s, "d_rc", 16 * NT)
                    ins = s.activation(g_sb[idx % 2][:, :], ps_f[idx % 3][:, :],
                                       AF.Silu, scale=r_col[:, tt:tt + 1]) if em else None
                    P.mark("s", f"s_silu{idx}", ins)

        def sy_stream(sy):
            em = P.emit

            def dma(sem, o, i):
                ins = sy.dma_start(out=o, in_=i) if em else None
                return P.inc(sem, ins, 16)

            dma("d_r0", xr_sb[0][:, :], xres[0:128, 0:NW])
            dma("d_r1", xr_sb[1][:, :], xres[128:256, 0:NW])
            P.wait(sy, "cc_done")
            dma("d_csr", csr[:, :], cc_out[:, :])
            for j in range(NCH):
                for tt in range(NT):
                    idx = 8 * j + tt
                    P.wait(sy, f"v_outadd{idx}")
                    dma(f"d_st{idx % 2}", out[tt * 128:(tt + 1) * 128, j * NW:(j + 1) * NW],
                        o_sb[idx % 2][:, :])
                    if idx + 2 < 64:
                        idx2 = idx + 2
                        j2, tt2 = idx2 // 8, idx2 % 8
                        dma(f"d_r{idx2 % 2}", xr_sb[idx2 % 2][:, :],
                            xres[tt2 * 128:(tt2 + 1) * 128, j2 * NW:(j2 + 1) * NW])
            P.wait_val(sy, "d_st0", 16 * 32)
            P.wait_val(sy, "d_st1", 16 * 32)

        class _Dummy:
            def __getattr__(self, name):
                def _noop(*a, **k):
                    return None
                return _noop

        P.emit = False
        d = _Dummy()
        gp_stream2(d)
        t_stream(d)
        v_stream(d)
        s_stream(d)
        sy_stream(d)
        P.cnt = {}
        P.emit = True

        @block.gpsimd
        def _(g):
            gp_stream2(g)

        @block.tensor
        def _(t):
            t_stream(t)

        @block.vector
        def _(v):
            v_stream(v)

        @block.scalar
        def _(s):
            s_stream(s)

        @block.sync
        def _(sy):
            sy_stream(sy)

    return nc


_NC_CACHE = None


def kernel(L0_attn_data, L0_attn_mask, L0_attn_weight, L0_attn_bias,
           L0_ffn_W_org, L0_ffn_lora_down, L0_ffn_lora_up, L0_ffn_lora_keys,
           L0_ffn_Wq):
    global _NC_CACHE
    x = np.asarray(L0_attn_data)
    B, S, _ = x.shape
    xr = x.reshape(B * S, D)

    idx = np.arange(D)
    perm = (idx % 128) * 32 + idx // 128

    wT = np.ascontiguousarray(np.asarray(L0_attn_weight).T[perm])
    orgT = np.ascontiguousarray(np.asarray(L0_ffn_W_org).T[perm])
    down = np.ascontiguousarray(
        np.asarray(L0_ffn_lora_down).transpose(1, 0, 2).reshape(D, ER)[perm])
    up = np.ascontiguousarray(np.asarray(L0_ffn_lora_up).reshape(ER, D))
    wqT = np.ascontiguousarray(np.asarray(L0_ffn_Wq).T[perm])
    keysT = np.ascontiguousarray(np.asarray(L0_ffn_lora_keys).T)
    bias = np.ascontiguousarray(np.asarray(L0_attn_bias).reshape(1, D))
    bmTs = np.ascontiguousarray(np.asarray(L0_attn_mask).T[::32, ::32])

    if _NC_CACHE is None:
        _NC_CACHE = build_program()
    nc = _NC_CACHE

    in_maps = []
    for c in range(8):
        tc = slice(c * TOK, (c + 1) * TOK)
        in_maps.append({
            "xT": np.ascontiguousarray(xr[tc].T[perm]),
            "xres": np.ascontiguousarray(xr[tc]),
            "wT": wT, "orgT": orgT, "down": down, "up": up,
            "wqT": wqT, "keysT": keysT, "bias": bias, "bmTs": bmTs,
        })
    res = run_bass_kernel_spmd(nc, in_maps, core_ids=list(range(8)))
    outs = [res.results[c]["out"] for c in range(8)]
    return np.concatenate(outs, axis=0).reshape(B, S, D).astype(np.float32)


# version 20 (final)
# speedup vs baseline: 1.0202x; 1.0202x over previous
import numpy as np
from contextlib import ExitStack

import concourse.bass as bass
import concourse.mybir as mybir
from concourse.bass_utils import run_bass_kernel_spmd

F32 = mybir.dt.float32
BF16 = mybir.dt.bfloat16
I32 = mybir.dt.int32
AF = mybir.ActivationFunctionType
ALU = mybir.AluOpType
AX = mybir.AxisListType

D = 4096
TOK = 1024
KT = 32
NT = 8
NCH = 8
NW = 512
ER = 512
EPS = 1e-5
SC_SCALE = 1.0 / (4096.0 * float(np.sqrt(128.0)))


class Plan:

    def __init__(self):
        self.ev = {}
        self.emit = False
        self.cnt = {}
        self.eng = {}
        self.sem = {}

    def inc(self, sem_name, instr, amount=1):
        self.cnt[sem_name] = self.cnt.get(sem_name, 0) + amount
        if self.emit and instr is not None:
            instr.then_inc(self.sem[sem_name], amount)
        return self.cnt[sem_name]

    def mark(self, sem_name, name, instr=None, amount=1):
        v = self.inc(sem_name, instr, amount)
        if not self.emit:
            self.ev[name] = (sem_name, v)
        return v

    def wait(self, eng, name):
        if self.emit:
            s, v = self.ev[name]
            eng.wait_ge(self.sem[s], v)

    def wait_val(self, eng, sem_name, v):
        if self.emit:
            eng.wait_ge(self.sem[sem_name], v)


def build_program():
    nc = bass.Bass("TRN2", target_bir_lowering=False)
    xT = nc.dram_tensor("xT", [D, TOK], F32, kind="ExternalInput").ap()
    xres = nc.dram_tensor("xres", [TOK, D], F32, kind="ExternalInput").ap()
    wT = nc.dram_tensor("wT", [D, D], F32, kind="ExternalInput").ap()
    orgT = nc.dram_tensor("orgT", [D, D], F32, kind="ExternalInput").ap()
    down = nc.dram_tensor("down", [D, ER], F32, kind="ExternalInput").ap()
    up = nc.dram_tensor("up", [ER, D], F32, kind="ExternalInput").ap()
    wqT = nc.dram_tensor("wqT", [D, 128], F32, kind="ExternalInput").ap()
    keysT = nc.dram_tensor("keysT", [128, 8], F32, kind="ExternalInput").ap()
    bias = nc.dram_tensor("bias", [1, D], F32, kind="ExternalInput").ap()
    bmTs = nc.dram_tensor("bmTs", [128, 128], I32, kind="ExternalInput").ap()
    out = nc.dram_tensor("out", [TOK, D], F32, kind="ExternalOutput").ap()
    cc_in = nc.dram_tensor("cc_in", [128, KT], F32)
    cc_out = nc.dram_tensor("cc_out", [128, KT], F32)

    xTr = xT.rearrange("(a p) t -> p a t", p=128)
    wTr = wT.rearrange("(a p) n -> p a n", p=128)
    orgTr = orgT.rearrange("(a p) n -> p a n", p=128)
    downr = down.rearrange("(a p) r -> p a r", p=128)
    upr = up.rearrange("(a p) n -> p a n", p=128)
    wqTr = wqT.rearrange("(a p) q -> p a q", p=128)

    P = Plan()
    ctx = ExitStack()
    with ctx:
        E = ctx.enter_context
        xc = E(nc.sbuf_tensor("xc", [128, KT * TOK], BF16))
        wA = E(nc.sbuf_tensor("wA", [128, KT * NW], BF16))
        wO = E(nc.sbuf_tensor("wO", [128, KT * NW], BF16))
        ub = [E(nc.sbuf_tensor(f"ub{i}", [128, 4 * NW], BF16)) for i in range(2)]
        db = [E(nc.sbuf_tensor(f"db{i}", [128, NW], BF16)) for i in range(3)]
        hz = E(nc.sbuf_tensor("hz", [128, 4 * TOK], BF16))
        bme = E(nc.sbuf_tensor("bme", [128, NW], BF16))
        bias_sm = E(nc.sbuf_tensor("bias_sm", [128, NW], BF16))
        m_row_bf = E(nc.sbuf_tensor("m_row_bf", [1, TOK], BF16))
        r_row_bf = E(nc.sbuf_tensor("r_row_bf", [1, TOK], BF16))
        co_row_bf = E(nc.sbuf_tensor("co_row_bf", [1, 8], BF16))
        wq_sb = E(nc.sbuf_tensor("wq_sb", [128, KT * 128], BF16))
        m_bc = E(nc.sbuf_tensor("m_bc", [128, TOK], BF16))
        r_bc = E(nc.sbuf_tensor("r_bc", [128, TOK], BF16))
        xr_sb = [E(nc.sbuf_tensor(f"xr{i}", [128, NW], F32)) for i in range(2)]
        s_sb = [E(nc.sbuf_tensor(f"ss{i}", [128, NW], BF16)) for i in range(8)]
        o_sb = [E(nc.sbuf_tensor(f"os{i}", [128, NW], F32)) for i in range(2)]
        g_sb = [E(nc.sbuf_tensor(f"gs{i}", [128, NW], F32)) for i in range(2)]
        cs = E(nc.sbuf_tensor("cs", [128, KT], F32))
        csr = E(nc.sbuf_tensor("csr", [128, KT], F32))
        csr_bf = E(nc.sbuf_tensor("csr_bf", [128, KT], BF16))
        r_col = E(nc.sbuf_tensor("r_col", [128, NT], F32))
        q_bf = E(nc.sbuf_tensor("q_bf", [128, 1], BF16))
        keys_f = E(nc.sbuf_tensor("keys_f", [128, 8], F32))
        keys_bf = E(nc.sbuf_tensor("keys_bf", [128, 8], BF16))
        bmT_i = E(nc.sbuf_tensor("bmT_i", [128, 128], I32))
        bmT_bf = E(nc.sbuf_tensor("bmT_bf", [128, 128], BF16))
        c8 = E(nc.sbuf_tensor("c8", [128, 8], F32))
        c_vec = E(nc.sbuf_tensor("c_vec", [128, 4], F32))
        ones_cb = E(nc.sbuf_tensor("ones_cb", [128, 1], BF16))
        bias_f = E(nc.sbuf_tensor("bias_f", [1, D], BF16))
        ones_rb = E(nc.sbuf_tensor("ones_rb", [1, 128], BF16))
        m_row = E(nc.sbuf_tensor("m_row", [1, TOK], F32))
        r_row = E(nc.sbuf_tensor("r_row", [1, TOK], F32))
        rowB = E(nc.sbuf_tensor("rowB", [1, TOK], F32))
        eps_sb = E(nc.sbuf_tensor("eps_sb", [1, 1], F32))
        m2_row = rowB
        var_row = m_row
        s0_row = rowB
        rec_row = r_row
        tn_row = rowB
        u_row = rowB
        s1_row = rowB
        sc_row = E(nc.sbuf_tensor("sc_row", [1, 8], F32))
        sc2_row = E(nc.sbuf_tensor("sc2_row", [1, 8], F32))
        mx_row = E(nc.sbuf_tensor("mx_row", [1, 1], F32))
        e_row = E(nc.sbuf_tensor("e_row", [1, 8], F32))
        sm_row = E(nc.sbuf_tensor("sm_row", [1, 1], F32))
        co_row = E(nc.sbuf_tensor("co_row", [1, 8], F32))
        ps = [E(nc.psum_tensor(f"ps{i}", [128, NW], F32)) for i in range(8)]
        ps_s = ps[0:3]
        ps_f = ps[3:6]
        ps_x = ps[6:8]
        sem_names = ["t", "v", "s", "g", "cc", "d_misc", "d_dn0", "d_dn1", "d_dn2",
                     "d_w", "d_o", "d_u0", "d_u1", "d_r0", "d_r1",
                     "d_st0", "d_st1", "d_cs", "d_csr", "d_rc", "d_wq",
                     "d_xc0", "d_xc1", "d_xc2", "d_xc3"]
        for sn in sem_names:
            P.sem[sn] = E(nc.semaphore(sn))
        block = E(nc.Block())

        def h_piece_ids():
            return list(range(32))

        def gp_stream2(g):
            em = P.emit

            def dma(sem, o, i):
                ins = g.dma_start(out=o, in_=i) if em else None
                return P.inc(sem, ins, 16)

            for qd in range(4):
                dma(f"d_xc{qd}", xc[:, qd * 8 * TOK:(qd + 1) * 8 * TOK],
                    xTr[:, qd * 8:(qd + 1) * 8, :])
            dma("d_misc", bmT_i[:, :], bmTs[:, :])
            dma("d_misc", bias_f[:, :], bias[:, :])
            dma("d_misc", keys_f[:, :], keysT[:, :])
            if not P.emit:
                P.ev["misc_ld"] = ("d_misc", P.cnt["d_misc"])
            for qd in range(4):
                dma("d_w", wA[:, qd * 8 * NW:(qd + 1) * 8 * NW],
                    wTr[:, qd * 8:(qd + 1) * 8, 0:NW])
            P.wait(g, "v_rrow")
            for tt in range(NT):
                dma("d_rc", r_col[:, tt:tt + 1], r_row[0:1, tt * 128:(tt + 1) * 128])
            for qd in range(4):
                dma("d_o", wO[:, qd * 8 * NW:(qd + 1) * 8 * NW],
                    orgTr[:, qd * 8:(qd + 1) * 8, 0:NW])
            dma("d_u0", ub[0][:, :], upr[:, :, 0:NW])
            dma("d_wq", wq_sb[:, :], wqTr[:, :, :])
            P.wait(g, "v_cs")
            dma("d_cs", cc_in[:, :], cs[:, :])
            P.wait_val(g, "d_cs", 16)
            ins = g.collective_compute(
                "AllReduce", ALU.add,
                replica_groups=[[0, 1, 2, 3], [4, 5, 6, 7]],
                ins=[cc_in[:, :].opt()],
                outs=[cc_out[:, :].opt()],
            ) if em else None
            P.mark("cc", "cc_done", ins)
            for i in h_piece_ids():
                if i >= 3:
                    P.wait(g, f"t_hpiece{i - 3}")
                dma(f"d_dn{i % 3}", db[i % 3][:, :],
                    downr[:, i:i + 1, :])
            for j in range(1, NCH):
                P.wait(g, f"t_attn_last{j - 1}")
                for qd in range(4):
                    dma("d_w", wA[:, qd * 8 * NW:(qd + 1) * 8 * NW],
                        wTr[:, qd * 8:(qd + 1) * 8, j * NW:(j + 1) * NW])
                P.wait(g, f"t_ffn_last{j - 1}")
                for qd in range(4):
                    dma("d_o", wO[:, qd * 8 * NW:(qd + 1) * 8 * NW],
                        orgTr[:, qd * 8:(qd + 1) * 8, j * NW:(j + 1) * NW])
                if j >= 2:
                    P.wait(g, f"t_ffn_last{j - 2}")
                dma(f"d_u{j % 2}", ub[j % 2][:, :], upr[:, :, j * NW:(j + 1) * NW])

        def t_stream(t):
            em = P.emit

            def mm(out_ap, lhsT, rhs, start, stop):
                return t.matmul(out_ap, lhsT, rhs, start=start, stop=stop) if em else None

            P.wait(t, "v_ones")
            for kt in range(KT):
                if kt % 8 == 0:
                    P.wait_val(t, f"d_xc{kt // 8}", 16)
                for c in range(2):
                    ins = mm(ps_x[c][0:1, 0:NW], ones_cb[:, :],
                             xc[:, kt * TOK + c * NW: kt * TOK + (c + 1) * NW],
                             kt == 0, kt == KT - 1)
                if kt == KT - 1:
                    P.mark("t", "t_msum", ins)
                P.wait(t, f"v_sq{kt}")
                for c in range(2):
                    ins = mm(ps_f[c][0:1, 0:NW], ones_cb[:, :],
                             hz[:, (kt % 4) * TOK + c * NW:(kt % 4) * TOK + (c + 1) * NW],
                             kt == 0, kt == KT - 1)
                P.mark("t", f"t_sq{kt}", ins)
            P.wait(t, "v_mrowbf")
            for c in range(2):
                ins = mm(ps_s[c][:, :], ones_rb[:, :], m_row_bf[0:1, c * NW:(c + 1) * NW],
                         True, True)
                P.mark("t", f"t_mbc{c}", ins)
            P.wait(t, "misc_ld")
            P.wait(t, "v_mrow")
            ins = mm(ps_x[0][:, :], ones_rb[:, :], bias_f[0:1, 0:NW], True, True)
            P.mark("t", "t_biasj0", ins)
            P.wait(t, "v_mask0")
            P.wait(t, "v_center_all")
            for tt in range(NT):
                idx = tt
                if tt == 2:
                    P.wait(t, "v_rrowbf")
                    P.wait(t, "v_mbc")
                    P.wait(t, "v_biasev0")
                    for c in range(2):
                        ins = mm(ps_x[c][:, :], ones_rb[:, :],
                                 r_row_bf[0:1, c * NW:(c + 1) * NW], True, True)
                        P.mark("t", f"t_rbc{c}", ins)
                if idx >= 3:
                    P.wait(t, f"v_sstt{idx - 3}")
                ins = None
                for kt in range(KT):
                    ins = mm(ps_s[idx % 3][:, :],
                             xc[:, kt * TOK + tt * 128: kt * TOK + (tt + 1) * 128],
                             wA[:, kt * NW:(kt + 1) * NW], kt == 0, kt == KT - 1)
                P.mark("t", f"t_attn_0_{tt}", ins)
            if not P.emit:
                P.ev["t_attn_last0"] = P.ev["t_attn_0_7"]
            P.wait(t, "v_sstt7")
            P.wait(t, "v_var")
            P.wait(t, "v_rbc")
            for i in h_piece_ids():
                P.wait_val(t, f"d_dn{i % 3}", 16 * (i // 3 + 1))
                kt = i
                ins = None
                for gg in range(8):
                    et = gg // 2
                    half = gg % 2
                    ins = mm(ps[gg][:, :],
                             db[i % 3][:, et * 128: et * 128 + 128],
                             xc[:, kt * TOK + half * NW: kt * TOK + (half + 1) * NW],
                             (i == 0), (i == 31))
                P.mark("t", f"t_hpiece{i}", ins)
            P.wait(t, "v_csrbf")
            P.wait(t, "v_hz_6")
            P.wait(t, "v_hz_7")
            P.wait(t, "v_biasev0")
            P.wait_val(t, "d_wq", 16)
            ins = None
            for kt in range(KT):
                ins = mm(ps_x[0][:, 0:1], wq_sb[:, kt * 128:(kt + 1) * 128],
                         csr_bf[:, kt:kt + 1], kt == 0, kt == KT - 1)
            P.mark("t", "t_q", ins)
            def org_group(tt):
                idx = tt
                if tt == 0:
                    P.wait_val(t, "d_o", 64)
                    P.wait_val(t, "d_u0", 16)
                P.wait(t, f"v_hz_{3 + tt}")
                ins = None
                for kt in range(KT):
                    ins = mm(ps_f[idx % 3][:, :],
                             xc[:, kt * TOK + tt * 128: kt * TOK + (tt + 1) * 128],
                             wO[:, kt * NW:(kt + 1) * NW], kt == 0, False)
                return ins

            org_group(0)
            P.wait(t, "v_qbf")
            ins = mm(ps_x[1][0:1, 0:8], q_bf[:, :], keys_bf[:, :], True, True)
            P.mark("t", "t_sc", ins)
            org_group(1)
            P.wait(t, "v_cobf")
            ins = mm(ps_x[0][:, 0:8], ones_rb[:, :], co_row_bf[:, :], True, True)
            P.mark("t", "t_c8", ins)
            org_group(2)
            P.wait(t, "v_zT")
            for tt in range(3):
                idx = tt
                ins = None
                for et in range(4):
                    ins = mm(ps_f[idx % 3][:, :],
                             hz[:, et * TOK + tt * 128: et * TOK + (tt + 1) * 128],
                             ub[0][:, et * NW:(et + 1) * NW], False, et == 3)
                P.mark("t", f"t_ffn_0_{tt}", ins)
            for j in range(NCH):
                if j >= 1:
                    if j == 1:
                        P.wait(t, "v_scrow")
                    elif j == 2:
                        P.wait(t, "v_c8")
                    else:
                        P.wait(t, f"v_biasev{j - 2}")
                    ins = mm(ps_x[j % 2][:, :], ones_rb[:, :],
                             bias_f[0:1, j * NW:(j + 1) * NW], True, True)
                    P.mark("t", f"t_biasj{j}", ins)
                    P.wait(t, f"v_mask{j}")
                    for tt in range(NT):
                        idx = 8 * j + tt
                        if j == 1 and tt < 3:
                            P.wait(t, f"v_hz_{(8 + tt) % 3}")
                        if idx >= 3:
                            P.wait(t, f"v_sstt{idx - 3}")
                        ins = None
                        for kt in range(KT):
                            ins = mm(ps_s[idx % 3][:, :],
                                     xc[:, kt * TOK + tt * 128: kt * TOK + (tt + 1) * 128],
                                     wA[:, kt * NW:(kt + 1) * NW], kt == 0, kt == KT - 1)
                        P.mark("t", f"t_attn_{j}_{tt}", ins)
                for tt in range(NT):
                    idx = 8 * j + tt
                    if j == 0 and tt < 3:
                        continue
                    if tt == 0:
                        P.wait_val(t, "d_o", 64 * (j + 1))
                        P.wait_val(t, f"d_u{j % 2}", 16 * (j // 2 + 1))
                    if idx >= 3:
                        P.wait(t, f"s_silu{idx - 3}")
                    ins = None
                    for kt in range(KT):
                        ins = mm(ps_f[idx % 3][:, :],
                                 xc[:, kt * TOK + tt * 128: kt * TOK + (tt + 1) * 128],
                                 wO[:, kt * NW:(kt + 1) * NW], kt == 0, False)
                    for et in range(4):
                        ins = mm(ps_f[idx % 3][:, :],
                                 hz[:, et * TOK + tt * 128: et * TOK + (tt + 1) * 128],
                                 ub[j % 2][:, et * NW:(et + 1) * NW], False, et == 3)
                    P.mark("t", f"t_ffn_{j}_{tt}", ins)
                if not P.emit:
                    if j >= 1:
                        P.ev[f"t_attn_last{j}"] = P.ev[f"t_attn_{j}_{NT - 1}"]
                    P.ev[f"t_ffn_last{j}"] = P.ev[f"t_ffn_{j}_{NT - 1}"]

        def v_stream(v):
            em = P.emit

            def vi(f, *a, **kw):
                return f(*a, **kw) if em else None

            ins = vi(v.memset, ones_cb[:, :], 1.0)
            ins = vi(v.memset, ones_rb[:, :], 1.0)
            ins = vi(v.memset, eps_sb[:, :], EPS)
            ins = vi(v.engine_nop) if em else None
            P.mark("v", "v_ones", ins)
            for kt in range(KT):
                if kt % 8 == 0:
                    P.wait_val(v, f"d_xc{kt // 8}", 16)
                if kt >= 4:
                    P.wait(v, f"t_sq{kt - 4}")
                ins = vi(v.tensor_tensor, hz[:, (kt % 4) * TOK:(kt % 4 + 1) * TOK],
                         xc[:, kt * TOK:(kt + 1) * TOK],
                         xc[:, kt * TOK:(kt + 1) * TOK], ALU.mult)
                P.mark("v", f"v_sq{kt}", ins)
            P.wait(v, "t_msum")
            for c in range(2):
                ins = vi(v.tensor_scalar, m_row[0:1, c * NW:(c + 1) * NW],
                         ps_x[c][0:1, 0:NW], 1.0 / 4096.0, None, ALU.mult)
            P.mark("v", "v_mrow", ins)
            P.wait(v, "v_mrow")
            ins = vi(v.tensor_copy, m_row_bf[:, :], m_row[:, :])
            P.mark("v", "v_mrowbf", ins)
            ins = vi(v.tensor_tensor, m2_row[:, :], m_row[:, :], m_row[:, :], ALU.mult)
            P.mark("v", "v_m2", ins)
            P.wait(v, f"t_sq{KT - 1}")
            P.wait(v, "t_mbc1")
            P.wait(v, "v_m2")
            for c in range(2):
                ins = vi(v.scalar_tensor_tensor, var_row[:, c * NW:(c + 1) * NW],
                         ps_f[c][0:1, 0:NW], 1.0 / 4096.0,
                         m2_row[:, c * NW:(c + 1) * NW], ALU.mult, ALU.subtract)
            P.mark("v", "v_var", ins)

            P.wait(v, "t_mbc0")
            P.wait(v, "t_mbc1")
            for c in range(2):
                ins = vi(v.tensor_copy, m_bc[:, c * NW:(c + 1) * NW], ps_s[c][:, :])
            P.mark("v", "v_mbc", ins)
            P.wait(v, "misc_ld")
            ins = vi(v.tensor_single_scalar, bmT_bf[:, :], bmT_i[:, :], 0, ALU.is_gt)
            P.mark("v", "v_bmt", ins)
            P.wait(v, "v_bmt")
            for jj in range(32):
                ins = vi(v.tensor_copy, bme[:, jj::32], bmT_bf[:, 0:16])
            P.mark("v", "v_bme", ins)
            ins = vi(v.tensor_copy, keys_bf[:, :], keys_f[:, :])
            P.mark("v", "v_keys", ins)
            P.wait(v, "v_mbc")
            for kt in range(KT):
                ins = vi(v.tensor_tensor, xc[:, kt * TOK:(kt + 1) * TOK],
                         xc[:, kt * TOK:(kt + 1) * TOK], m_bc[:, :], ALU.subtract)
            P.mark("v", "v_center_all", ins)
            P.wait(v, "v_bme")
            P.wait_val(v, "d_w", 64)
            for kt in range(KT):
                ins = vi(v.tensor_tensor, wA[:, kt * NW:(kt + 1) * NW],
                         wA[:, kt * NW:(kt + 1) * NW], bme[:, :], ALU.mult)
            P.mark("v", "v_mask0", ins)
            P.wait(v, "s_sqrt")
            ins = vi(v.reciprocal, rec_row[:, :], s0_row[:, :])
            P.mark("v", "v_rec", ins)
            P.wait(v, "v_rec")
            ins = vi(v.tensor_tensor, tn_row[:, :], rec_row[:, :], rec_row[:, :], ALU.mult)
            P.mark("v", "v_tn", ins)
            P.wait(v, "v_tn")
            ins = vi(v.scalar_tensor_tensor, u_row[:, :], var_row[:, :], EPS,
                     tn_row[:, :], ALU.add, ALU.mult)
            P.mark("v", "v_u", ins)
            P.wait(v, "v_u")
            ins = vi(v.tensor_scalar, s1_row[:, :], u_row[:, :], -0.5, 1.5, ALU.mult, ALU.add)
            P.mark("v", "v_s1", ins)
            P.wait(v, "v_s1")
            ins = vi(v.tensor_tensor, r_row[:, :], rec_row[:, :], s1_row[:, :], ALU.mult)
            P.mark("v", "v_rrow", ins)
            P.wait(v, "v_rrow")
            ins = vi(v.tensor_copy, r_row_bf[:, :], r_row[:, :])
            P.mark("v", "v_rrowbf", ins)
            P.wait(v, "t_biasj0")
            ins = vi(v.tensor_copy, bias_sm[:, :], ps_x[0][:, :])
            P.mark("v", "v_biasev0", ins)
            P.wait(v, "v_biasev0")

            def sstt0(tt):
                P.wait(v, f"t_attn_0_{tt}")
                if tt == 0:
                    P.wait_val(v, "d_rc", 16 * NT)
                ins = vi(v.scalar_tensor_tensor, s_sb[tt % 8][:, :],
                         ps_s[tt % 3][:, :], r_col[:, tt:tt + 1],
                         bias_sm[:, :], ALU.mult, ALU.add)
                P.mark("v", f"v_sstt{tt}", ins)

            sstt0(0)
            sstt0(1)
            P.wait(v, "t_rbc0")
            P.wait(v, "t_rbc1")
            for c in range(2):
                ins = vi(v.tensor_copy, r_bc[:, c * NW:(c + 1) * NW], ps_x[c][:, :])
            P.mark("v", "v_rbc", ins)
            P.wait(v, "v_rbc")
            for kt in range(KT):
                if kt >= 2:
                    P.wait(v, f"v_cs{kt - 2}")
                ins = vi(v.scalar_tensor_tensor, hz[:, (2 + kt % 2) * TOK:(3 + kt % 2) * TOK],
                         xc[:, kt * TOK:(kt + 1) * TOK], 1.0, r_bc[:, :],
                         ALU.mult, ALU.mult, accum_out=cs[:, kt:kt + 1])
                P.mark("v", f"v_cs{kt}", ins)
                if kt % 8 == 7 and kt // 8 + 2 <= 7:
                    sstt0(kt // 8 + 2)
            if not P.emit:
                P.ev["v_cs"] = P.ev[f"v_cs{KT - 1}"]
            for tt in range(6, NT):
                sstt0(tt)
            P.wait_val(v, "d_csr", 16)
            ins = vi(v.tensor_copy, csr_bf[:, :], csr[:, :])
            P.mark("v", "v_csrbf", ins)
            P.wait(v, "t_hpiece31")
            P.wait(v, "v_cs")
            for gg in range(8):
                et = gg // 2
                half = gg % 2
                ins = vi(v.tensor_copy, hz[:, et * TOK + half * NW: et * TOK + (half + 1) * NW],
                         ps[gg][:, :])
                P.mark("v", f"v_hz_{gg}", ins)
            P.wait(v, "t_q")
            ins = vi(v.tensor_copy, q_bf[:, :], ps_x[0][:, 0:1])
            P.mark("v", "v_qbf", ins)
            P.wait(v, "t_sc")
            ins = vi(v.tensor_scalar, sc_row[:, :], ps_x[1][0:1, 0:8], SC_SCALE, None, ALU.mult)
            P.mark("v", "v_scrow", ins)
            P.wait(v, "v_scrow")
            ins = vi(v.reduce_max, mx_row[:, :], sc_row[:, :], AX.X)
            P.mark("v", "v_mx", ins)
            P.wait(v, "v_mx")
            ins = vi(v.tensor_scalar, sc2_row[:, :], sc_row[:, :], mx_row[0:1, 0:1], None, ALU.subtract)
            P.mark("v", "v_sc2", ins)
            P.wait(v, "s_exp")
            ins = vi(v.reduce_sum, sm_row[:, :], e_row[:, :], AX.X)
            P.mark("v", "v_sm", ins)
            P.wait(v, "v_sm")
            ins = vi(v.reciprocal, sm_row[:, :], sm_row[:, :])
            P.mark("v", "v_smr", ins)
            P.wait(v, "v_smr")
            ins = vi(v.tensor_scalar, co_row[:, :], e_row[:, :], sm_row[0:1, 0:1], None, ALU.mult)
            P.mark("v", "v_coeffs", ins)
            P.wait(v, "v_coeffs")
            ins = vi(v.tensor_copy, co_row_bf[:, :], co_row[:, :])
            P.mark("v", "v_cobf", ins)
            P.wait(v, "t_c8")
            ins = vi(v.tensor_copy, c8[:, :], ps_x[0][:, 0:8])
            P.mark("v", "v_c8", ins)
            P.wait(v, "v_c8")
            for et in range(4):
                ins = vi(v.tensor_copy, c_vec[0:64, et:et + 1], c8[0:64, 2 * et:2 * et + 1])
                ins = vi(v.tensor_copy, c_vec[64:128, et:et + 1], c8[64:128, 2 * et + 1:2 * et + 2])
            P.mark("v", "v_cvec", ins)
            P.wait(v, "v_cvec")
            for et in range(4):
                P.wait(v, f"v_hz_{2 * et}")
                P.wait(v, f"v_hz_{2 * et + 1}")
                ins = vi(v.scalar_tensor_tensor, hz[:, et * TOK:(et + 1) * TOK],
                         hz[:, et * TOK:(et + 1) * TOK], c_vec[:, et:et + 1],
                         r_bc[:, :], ALU.mult, ALU.mult)
            P.mark("v", "v_zT", ins)
            P.wait_val(v, "d_w", 128)
            P.wait(v, "v_mask0")
            for jj in range(32):
                ins = vi(v.tensor_copy, bme[:, jj::32], bmT_bf[:, 16:32])
            ins = vi(v.engine_nop) if P.emit else None
            P.mark("v", "v_bme1", ins)
            P.wait(v, "v_bme1")
            for kt in range(KT):
                ins = vi(v.tensor_tensor, wA[:, kt * NW:(kt + 1) * NW],
                         wA[:, kt * NW:(kt + 1) * NW], bme[:, :], ALU.mult)
            P.mark("v", "v_mask1", ins)
            for j in range(NCH):
                if j >= 1:
                    P.wait(v, f"t_biasj{j}")
                    P.wait(v, f"v_sstt{8 * (j - 1) + 7}")
                    ins = vi(v.tensor_copy, bias_sm[:, :], ps_x[j % 2][:, :])
                    P.mark("v", f"v_biasev{j}", ins)
                    P.wait(v, f"v_biasev{j}")
                    for tt in range(NT):
                        idx = 8 * j + tt
                        P.wait(v, f"t_attn_{j}_{tt}")
                        if idx >= 8:
                            P.wait(v, f"v_tmul{idx - 8}")
                        ins = vi(v.scalar_tensor_tensor, s_sb[idx % 8][:, :],
                                 ps_s[idx % 3][:, :], r_col[:, tt:tt + 1],
                                 bias_sm[:, :], ALU.mult, ALU.add)
                        P.mark("v", f"v_sstt{idx}", ins)

                def epi(tt):
                    idx = 8 * j + tt
                    P.wait(v, f"s_silu{idx}")
                    P.wait(v, f"v_sstt{idx}")
                    if idx >= 2:
                        P.wait_val(v, f"d_st{idx % 2}", 16 * (idx // 2))
                    ins = vi(v.tensor_tensor, o_sb[idx % 2][:, :], s_sb[idx % 8][:, :],
                             g_sb[idx % 2][:, :], ALU.mult)
                    P.mark("v", f"v_tmul{idx}", ins)
                    P.wait(v, f"v_tmul{idx}")
                    P.wait_val(v, f"d_r{idx % 2}", 16 * (idx // 2 + 1))
                    ins = vi(v.tensor_tensor, o_sb[idx % 2][:, :], o_sb[idx % 2][:, :],
                             xr_sb[idx % 2][:, :], ALU.add)
                    P.mark("v", f"v_outadd{idx}", ins)

                for tt in range(4):
                    epi(tt)
                if j + 1 < NCH and j >= 1:
                    P.wait_val(v, "d_w", 64 * (j + 2))
                    P.wait(v, f"v_mask{j}")
                    for jj in range(32):
                        ins = vi(v.tensor_copy, bme[:, jj::32],
                                 bmT_bf[:, (j + 1) * 16:(j + 2) * 16])
                    ins = vi(v.engine_nop) if P.emit else None
                    P.mark("v", f"v_bme{j + 1}", ins)
                    P.wait(v, f"v_bme{j + 1}")
                    for kt in range(KT):
                        ins = vi(v.tensor_tensor, wA[:, kt * NW:(kt + 1) * NW],
                                 wA[:, kt * NW:(kt + 1) * NW],
                                 bme[:, :], ALU.mult)
                    P.mark("v", f"v_mask{j + 1}", ins)
                for tt in range(4, NT):
                    epi(tt)

        def s_stream(s):
            em = P.emit
            P.wait(s, "v_var")
            ins = s.activation(s0_row[:, :], var_row[:, :], AF.Sqrt,
                               bias=eps_sb[0:1, 0:1]) if em else None
            P.mark("s", "s_sqrt", ins)
            P.wait(s, "v_sc2")
            ins = s.activation(e_row[:, :], sc2_row[:, :], AF.Exp) if em else None
            P.mark("s", "s_exp", ins)
            for j in range(NCH):
                for tt in range(NT):
                    idx = 8 * j + tt
                    P.wait(s, f"t_ffn_{j}_{tt}")
                    if idx >= 2:
                        P.wait(s, f"v_tmul{idx - 2}")
                    if idx == 0:
                        P.wait_val(s, "d_rc", 16 * NT)
                    ins = s.activation(g_sb[idx % 2][:, :], ps_f[idx % 3][:, :],
                                       AF.Silu, scale=r_col[:, tt:tt + 1]) if em else None
                    P.mark("s", f"s_silu{idx}", ins)

        def sy_stream(sy):
            em = P.emit

            def dma(sem, o, i):
                ins = sy.dma_start(out=o, in_=i) if em else None
                return P.inc(sem, ins, 16)

            dma("d_r0", xr_sb[0][:, :], xres[0:128, 0:NW])
            dma("d_r1", xr_sb[1][:, :], xres[128:256, 0:NW])
            P.wait(sy, "cc_done")
            dma("d_csr", csr[:, :], cc_out[:, :])
            for j in range(NCH):
                for tt in range(NT):
                    idx = 8 * j + tt
                    P.wait(sy, f"v_outadd{idx}")
                    dma(f"d_st{idx % 2}", out[tt * 128:(tt + 1) * 128, j * NW:(j + 1) * NW],
                        o_sb[idx % 2][:, :])
                    if idx + 2 < 64:
                        idx2 = idx + 2
                        j2, tt2 = idx2 // 8, idx2 % 8
                        dma(f"d_r{idx2 % 2}", xr_sb[idx2 % 2][:, :],
                            xres[tt2 * 128:(tt2 + 1) * 128, j2 * NW:(j2 + 1) * NW])
            P.wait_val(sy, "d_st0", 16 * 32)
            P.wait_val(sy, "d_st1", 16 * 32)

        class _Dummy:
            def __getattr__(self, name):
                def _noop(*a, **k):
                    return None
                return _noop

        P.emit = False
        d = _Dummy()
        gp_stream2(d)
        t_stream(d)
        v_stream(d)
        s_stream(d)
        sy_stream(d)
        P.cnt = {}
        P.emit = True

        @block.gpsimd
        def _(g):
            gp_stream2(g)

        @block.tensor
        def _(t):
            t_stream(t)

        @block.vector
        def _(v):
            v_stream(v)

        @block.scalar
        def _(s):
            s_stream(s)

        @block.sync
        def _(sy):
            sy_stream(sy)

    return nc


_NC_CACHE = None


def kernel(L0_attn_data, L0_attn_mask, L0_attn_weight, L0_attn_bias,
           L0_ffn_W_org, L0_ffn_lora_down, L0_ffn_lora_up, L0_ffn_lora_keys,
           L0_ffn_Wq):
    global _NC_CACHE
    x = np.asarray(L0_attn_data)
    B, S, _ = x.shape
    xr = x.reshape(B * S, D)

    idx = np.arange(D)
    perm = (idx % 128) * 32 + idx // 128

    wT = np.ascontiguousarray(np.asarray(L0_attn_weight).T[perm])
    orgT = np.ascontiguousarray(np.asarray(L0_ffn_W_org).T[perm])
    down = np.ascontiguousarray(
        np.asarray(L0_ffn_lora_down).transpose(1, 0, 2).reshape(D, ER)[perm])
    up = np.ascontiguousarray(np.asarray(L0_ffn_lora_up).reshape(ER, D))
    wqT = np.ascontiguousarray(np.asarray(L0_ffn_Wq).T[perm])
    keysT = np.ascontiguousarray(np.asarray(L0_ffn_lora_keys).T)
    bias = np.ascontiguousarray(np.asarray(L0_attn_bias).reshape(1, D))
    bmTs = np.ascontiguousarray(np.asarray(L0_attn_mask).T[::32, ::32])

    if _NC_CACHE is None:
        _NC_CACHE = build_program()
    nc = _NC_CACHE

    in_maps = []
    for c in range(8):
        tc = slice(c * TOK, (c + 1) * TOK)
        in_maps.append({
            "xT": np.ascontiguousarray(xr[tc].T[perm]),
            "xres": np.ascontiguousarray(xr[tc]),
            "wT": wT, "orgT": orgT, "down": down, "up": up,
            "wqT": wqT, "keysT": keysT, "bias": bias, "bmTs": bmTs,
        })
    res = run_bass_kernel_spmd(nc, in_maps, core_ids=list(range(8)))
    outs = [res.results[c]["out"] for c in range(8)]
    return np.concatenate(outs, axis=0).reshape(B, S, D).astype(np.float32)
